# version 09
# baseline (speedup 1.0000x reference)
import os
os.environ.setdefault("BASS_CLEANUP_STALE_RUNS", "1")
import contextlib
import numpy as np
import ml_dtypes
import concourse.bass as bass
import concourse.mybir as mybir
from concourse.bass_utils import run_bass_kernel_spmd

F32 = mybir.dt.float32
BF16 = mybir.dt.bfloat16
I32 = mybir.dt.int32
AF = mybir.ActivationFunctionType
ALU = mybir.AluOpType
AX = mybir.AxisListType

T, TO, D = 4096, 2048, 1024
NB, TB, NCH = 4, 1024, 8
H, DH, M, IF = 16, 64, 256, 2816
ENGS = ("sync", "scalar", "vector", "tensor", "gpsimd")
QW = 2816


class Ctr:
    __slots__ = ("v",)

    def __init__(self):
        self.v = 0

    def add(self, k=1):
        self.v += k
        return self.v


def build_program(upto="G", debug=False, skip_w=False):
    nc = bass.Bass("TRN2", target_bir_lowering=False)
    ap = lambda t: t.ap()
    x_in = ap(nc.dram_tensor("x_in", [T, D], F32, kind="ExternalInput"))
    wqkv = ap(nc.dram_tensor("wqkv", [D, 3 * D], F32, kind="ExternalInput"))
    bqkv = ap(nc.dram_tensor("bqkv", [3 * D], F32, kind="ExternalInput"))
    wout = ap(nc.dram_tensor("wout", [D, D], F32, kind="ExternalInput"))
    bout = ap(nc.dram_tensor("bout", [D], F32, kind="ExternalInput"))
    omega = ap(nc.dram_tensor("omega", [DH, M], F32, kind="ExternalInput"))
    w_gate = ap(nc.dram_tensor("w_gate", [IF, D], F32, kind="ExternalInput"))
    w_up = ap(nc.dram_tensor("w_up", [IF, D], F32, kind="ExternalInput"))
    w_down = ap(nc.dram_tensor("w_down", [D, IF], F32, kind="ExternalInput"))
    wq1 = ap(nc.dram_tensor("wq1", [D, D], F32, kind="ExternalInput"))
    wk1 = ap(nc.dram_tensor("wk1", [D, D], F32, kind="ExternalInput"))
    wv1 = ap(nc.dram_tensor("wv1", [D, D], F32, kind="ExternalInput"))
    w1 = ap(nc.dram_tensor("w1", [D, D], F32, kind="ExternalInput"))
    b1 = ap(nc.dram_tensor("b1", [D], F32, kind="ExternalInput"))
    l_dn = ap(nc.dram_tensor("l_dn", [4, 100], F32, kind="ExternalInput"))
    l_up = ap(nc.dram_tensor("l_up", [50, 4], F32, kind="ExternalInput"))
    d_aux = ap(nc.dram_tensor("d_aux", [100, D], F32, kind="ExternalInput"))
    u_aux = ap(nc.dram_tensor("u_aux", [D, 50], F32, kind="ExternalInput"))
    y_out = ap(nc.dram_tensor("y_out", [TO, D], F32, kind="ExternalOutput"))

    dscr = lambda n, s, dt=BF16: ap(nc.dram_tensor(n, s, dt, kind="Internal"))
    s_xn = dscr("s_xn", [T, D])
    s_xq = dscr("s_xq", [T, D])
    s_q0 = dscr("s_q0", [D, T])
    s_k0 = dscr("s_k0", [D, T])
    s_v0 = dscr("s_v0", [T, D])
    s_o0 = dscr("s_o0", [T, D])
    s_hq = dscr("s_hq", [TB, IF])
    s_x1 = dscr("s_x1", [TO, D], F32)
    s_xn1 = dscr("s_xn1", [T, D])
    s_k1 = dscr("s_k1", [D, T])
    s_v1 = dscr("s_v1", [T, D])
    s_wg = dscr("s_wg", [IF, D])
    s_wu = dscr("s_wu", [IF, D])
    s_wd = dscr("s_wd", [D, IF])
    s_wgt = dscr("s_wgt", [D, IF])
    s_wut = dscr("s_wut", [D, IF])
    s_wdt = dscr("s_wdt", [IF, D])
    s_wq1 = dscr("s_wq1", [D, D])
    s_wk1 = dscr("s_wk1", [D, D])
    s_wv1 = dscr("s_wv1", [D, D])
    s_w1 = dscr("s_w1", [D, D])

    st = contextlib.ExitStack()
    sb = lambda n, s, dt: st.enter_context(nc.sbuf_tensor(n, s, dt))
    sem = lambda n: st.enter_context(nc.semaphore(n))

    bv_row = sb("bv_row", [1, D], BF16)
    bo_row = sb("bo_row", [1, D], BF16)
    b1_row = sb("b1_row", [1, D], BF16)
    ones_r = sb("ones_r", [1, 128], BF16)
    ones_r5 = sb("ones_r5", [1, 512], BF16)
    ones_c = sb("ones_c", [128, 1], BF16)
    identz = sb("identz", [128, 128], I32)
    identf = sb("identf", [128, 128], F32)
    omx_o = sb("omx_o", [64, 2, 128], BF16)
    negh = sb("negh", [64, 128], BF16)
    rhk_o = sb("rhk_o", [64, 257], BF16)
    rhk_s = sb("rhk_s", [64, 257], BF16)
    stQ = contextlib.ExitStack()
    sbQ = lambda n, s, dt: stQ.enter_context(nc.sbuf_tensor(n, s, dt))
    qt_a = sbQ("qt_a", [128, QW], F32)
    qt_sg = sbQ("qt_sg", [128, QW], BF16)
    qt_r1 = sbQ("qt_r1", [128, QW], I32)
    qt_h1 = sbQ("qt_h1", [128, QW], BF16)
    qt_t2 = sbQ("qt_t2", [128, QW], BF16)
    qt_c6 = sbQ("qt_c6", [128, QW], BF16)
    q_mx = sbQ("q_mx", [128, 256], F32)
    q_sc = sbQ("q_sc", [128, 256], F32)
    q_im = sbQ("q_im", [128, 256], F32)
    q_scb = sbQ("q_scb", [128, 256], BF16)
    bnst = sbQ("bnst", [128, 12], F32)
    mv8 = sbQ("mv8", [128, 8, 2], F32)
    r8 = sbQ("r8", [128, 8], F32)
    nm8 = sbQ("nm8", [128, 8], F32)

    NRING = 4

    class Queue:
        def __init__(self, name):
            self.name = name
            self.sems = [st.enter_context(nc.semaphore(f"{name}{i}")) for i in range(NRING)]
            self.uses = [0] * NRING
            self.n = 0

    class Ticket:
        __slots__ = ("q", "idx")

        def __init__(self, q, idx):
            self.q, self.idx = q, idx

        def __sub__(self, k):
            assert k % 16 == 0
            return Ticket(self.q, self.idx - k // 16)

        def __gt__(self, other):
            return self.idx > 0

    QS = Queue("dS")
    QA = Queue("dA")
    QG = Queue("dG")
    dS, dA, dG = QS, QA, QG
    cS = type("B", (), {})(); cA = type("B", (), {})(); cG = type("B", (), {})()
    sV, cV = sem("sV"), Ctr()
    sC, cC = sem("sC"), Ctr()
    sT, cT = sem("sT"), Ctr()
    sGp, cGp = sem("sGp"), Ctr()
    dOUT = sem("dOUT")

    E = {k: [] for k in ENGS}
    emit = lambda eng, fn: E[eng].append(fn)
    DSEM = {"sync": QS, "scalar": QA, "gpsimd": QG}
    PROG = {"vector": (sV, cV), "scalar": (sC, cC), "tensor": (sT, cT),
            "gpsimd": (sGp, cGp)}

    class _V:
        def __init__(self, q):
            self.q = q

        @property
        def v(self):
            return self.q.n * 16

    cS = _V(QS); cA = _V(QA); cG = _V(QG)

    def dma(eng, out, in_, transpose=False, slow=False):
        q = DSEM[eng]
        idx = q.n % NRING
        q.n += 1
        u = q.uses[idx]
        s = q.sems[idx]
        if u:
            emit(eng, lambda e, s=s, v=16 * u: e.wait_ge(s, v))
        q.uses[idx] = u + 1
        if transpose:
            emit(eng, lambda e, o=out, i=in_: e.dma_start_transpose(out=o, in_=i).then_inc(s, 16))
        elif slow:
            emit(eng, lambda e, o=out, i=in_: e.dma_start(out=o, in_=i, allow_slow_non_contiguous=True).then_inc(s, 16))
        else:
            emit(eng, lambda e, o=out, i=in_: e.dma_start(out=o, in_=i).then_inc(s, 16))
        return Ticket(q, q.n)

    def wait(eng, semh, val):
        if isinstance(semh, Queue):
            q = semh
            if isinstance(val, Ticket):
                j = val.idx
                if j <= 0:
                    return
                idx = (j - 1) % NRING
                tgt = 16 * ((j - 1) // NRING + 1)
                emit(eng, lambda e, s=q.sems[idx], v=tgt: e.wait_ge(s, v))
            else:
                J = int(val) // 16
                if J <= 0:
                    return
                for idx in range(NRING):
                    cnt = (J - idx + NRING - 1) // NRING if J > idx else 0
                    if cnt > 0:
                        emit(eng, lambda e, s=q.sems[idx], v=16 * cnt: e.wait_ge(s, v))
            return
        if isinstance(val, Ticket):
            raise AssertionError("ticket wait on non-queue sem")
        if val > 0:
            emit(eng, lambda e, s=semh, v=val: e.wait_ge(s, v))

    def vdr():
        emit("vector", lambda e: e.drain())

    def sdr():
        emit("scalar", lambda e: e.drain())

    def inc(eng):
        s, c = PROG[eng]
        v = c.add(1)
        if eng in ("vector", "scalar", "gpsimd", "tensor"):
            emit(eng, lambda e: e.drain())
        emit(eng, lambda e, s=s: e.nop().then_inc(s, 1))
        return v

    def barrier():
        vals = {eng: inc(eng) for eng in ("vector", "scalar", "tensor", "gpsimd")}
        for eng in ENGS:
            for peng in ("vector", "scalar", "tensor", "gpsimd"):
                if peng != eng:
                    wait(eng, PROG[peng][0], vals[peng])
            wait(eng, dS, cS.v)
            wait(eng, dA, cA.v)
            wait(eng, dG, cG.v)

    class Pool:
        def __init__(self, tensors):
            self.t = tensors
            self.free_at = [None] * len(tensors)
            self.i = 0

        def get(self, user_eng):
            k = self.i % len(self.t)
            self.i += 1
            if self.free_at[k] is not None:
                eng, val = self.free_at[k]
                wait(user_eng, PROG[eng][0], val)
            return k, self.t[k]

        def release(self, k, eng, val):
            self.free_at[k] = (eng, val)

    dbg_outs = []

    def _dbg(name, src_ap):
        if not debug:
            return
        o = ap(nc.dram_tensor("dbg_" + name, list(src_ap.shape), src_ap.dtype, kind="ExternalOutput"))
        wait("gpsimd", dS, cS.v)
        wait("gpsimd", dA, cA.v)
        emit("gpsimd", lambda e, o=o, i=src_ap: e.dma_start(out=o, in_=i).then_inc(dOUT, 16))
        dbg_outs.append(name)

    def _assemble(stacks):
        if debug:
            n = len(dbg_outs) * 16
            if n:
                wait("gpsimd", dOUT, n)
                wait("sync", dOUT, n)
        with nc.allow_low_precision(reason="bf16 quant/evac by design"), nc.Block() as block:
            @block.sync
            def _(e):
                for f in E["sync"]:
                    f(e)

            @block.scalar
            def _(e):
                for f in E["scalar"]:
                    f(e)

            @block.vector
            def _(e):
                for f in E["vector"]:
                    f(e)

            @block.tensor
            def _(e):
                for f in E["tensor"]:
                    f(e)

            @block.gpsimd
            def _(e):
                for f in E["gpsimd"]:
                    f(e)
        for s in stacks:
            s.close()
        return nc

    KNOCONST = os.environ.get("KNOCONST")
    _, cgp = PROG["gpsimd"]
    g_iota = cgp.add(1)
    if not KNOCONST:
        emit("gpsimd", lambda e: e.iota(identz[:, :], pattern=[[-1, 128]], channel_multiplier=1).then_inc(sGp, 1))
    else:
        emit("gpsimd", lambda e: e.nop().then_inc(sGp, 1))
    wait("vector", sGp, g_iota)
    if not KNOCONST:
        emit("vector", lambda e: e.tensor_scalar(identf[:, :], identz[:, :], 0, None, ALU.is_equal))
    emit("vector", lambda e: e.memset(ones_r[:1, :], 1.0))
    emit("vector", lambda e: e.memset(ones_r5[:1, :], 1.0))
    emit("vector", lambda e: e.memset(ones_c[:, :], 1.0))
    emit("vector", lambda e: e.memset(negh[:, :], -0.5))
    emit("vector", lambda e: e.memset(rhk_o[:, :], 0.0))
    emit("vector", lambda e: e.memset(rhk_s[:, :], 0.0))
    vdr()
    emit("vector", lambda e: e.memset(rhk_s[:, 256:257], -0.5))
    v_const = inc("vector")

    if not KNOCONST:
        dma("gpsimd", bv_row[:1, :], bqkv[2048:3072].unsqueeze(0))
        dma("gpsimd", bo_row[:1, :], bout.unsqueeze(0))
        dma("gpsimd", b1_row[:1, :], b1.unsqueeze(0))
        wait("gpsimd", sV, v_const)
        dma("gpsimd", omx_o[:, :, :], omega.rearrange("h (c m) -> h c m", c=2))
        dma("gpsimd", rhk_o[:, 0:256], omega)
        dma("gpsimd", s_wq1, wq1)
        dma("gpsimd", s_wk1, wk1)
        dma("gpsimd", s_wv1, wv1)
        d_smalls = dma("gpsimd", s_w1, w1)
    else:
        d_smalls = Ticket(QG, 0)

    q_lastv = [0]

    def quant(src, dst, W):
        nb = W // 16
        imb = q_im[:, 0:nb].unsqueeze(2).broadcast_to([128, nb, 16])
        scb = q_scb[:, 0:nb].unsqueeze(2).broadcast_to([128, nb, 16])
        sgm = lambda t: t[:, 0:W].rearrange("p (nb b) -> p nb b", b=16)
        emit("vector", lambda e: e.reduce_max(q_mx[:, 0:nb], src.rearrange("p (nb b) -> p nb b", b=16), axis=AX.X, apply_absolute_value=True))
        vdr()
        emit("vector", lambda e: e.tensor_scalar(q_sc[:, 0:nb], q_mx[:, 0:nb], 1.0 / 6.0, 1e-12, ALU.mult, ALU.max))
        vdr()
        emit("vector", lambda e: e.reciprocal(q_im[:, 0:nb], q_sc[:, 0:nb]))
        vdr()
        emit("vector", lambda e: e.tensor_scalar(q_scb[:, 0:nb], q_sc[:, 0:nb], 1.0, None, ALU.mult))
        vq = inc("vector")
        wait("scalar", sV, max(vq, q_lastv[0]))
        emit("scalar", lambda e: e.activation(qt_a[:, 0:W], src, AF.Abs, bias=0.0, scale=1.0))
        sdr()
        emit("scalar", lambda e: e.activation(qt_sg[:, 0:W], src, AF.Sign, bias=0.0, scale=1.0))
        sq = inc("scalar")
        wait("vector", sC, sq)
        emit("vector", lambda e: e.tensor_tensor(sgm(qt_a), sgm(qt_a), imb, ALU.mult))
        vdr()
        emit("vector", lambda e: e.tensor_scalar(qt_r1[:, 0:W], qt_a[:, 0:W], 2.0, None, ALU.mult))
        vdr()
        emit("vector", lambda e: e.tensor_scalar(qt_h1[:, 0:W], qt_r1[:, 0:W], 4, None, ALU.min))
        vdr()
        emit("vector", lambda e: e.tensor_scalar(qt_r1[:, 0:W], qt_a[:, 0:W], 1.0, None, ALU.mult))
        vdr()
        emit("vector", lambda e: e.tensor_scalar(qt_t2[:, 0:W], qt_r1[:, 0:W], 2, 4, ALU.max, ALU.min))
        vdr()
        emit("vector", lambda e: e.tensor_scalar(qt_c6[:, 0:W], qt_a[:, 0:W], 5.0, 2.0, ALU.is_gt, ALU.mult))
        vdr()
        emit("vector", lambda e: e.scalar_tensor_tensor(qt_t2[:, 0:W], qt_h1[:, 0:W], 0.5, qt_t2[:, 0:W], ALU.mult, ALU.add))
        vdr()
        emit("vector", lambda e: e.scalar_tensor_tensor(qt_t2[:, 0:W], qt_t2[:, 0:W], -2.0, qt_c6[:, 0:W], ALU.add, ALU.add))
        vdr()
        emit("vector", lambda e: e.tensor_tensor(qt_t2[:, 0:W], qt_t2[:, 0:W], qt_sg[:, 0:W], ALU.mult))
        vdr()
        emit("vector", lambda e: e.tensor_tensor(dst, sgm(qt_t2), scb, ALU.mult))
        vdr()
        v_end = inc("vector")
        q_lastv[0] = v_end
        return v_end

    def layer_norm_block(xsrc, xn_dst, sqtmp):
        for a in range(NCH):
            emit("vector", lambda e, a=a: e.reduce_sum(mv8[:, a, 0:1], xsrc[:, a, :], axis=AX.X))
            vdr()
            emit("vector", lambda e, a=a: e.tensor_tensor(sqtmp[:, 0:1024], xsrc[:, a, :], xsrc[:, a, :], ALU.mult))
            vdr()
            emit("vector", lambda e, a=a: e.reduce_sum(mv8[:, a, 1:2], sqtmp[:, 0:1024], axis=AX.X))
            vdr()
        emit("vector", lambda e: e.tensor_scalar(r8[:, :], mv8[:, :, 0], 1.0 / 1024.0, None, ALU.mult))
        vdr()
        emit("vector", lambda e: e.scalar_tensor_tensor(nm8[:, :], r8[:, :], -1.0, r8[:, :], ALU.mult, ALU.mult))
        vdr()
        emit("vector", lambda e: e.tensor_scalar(mv8[:, :, 0], r8[:, :], 1.0, None, ALU.mult))
        vdr()
        emit("vector", lambda e: e.scalar_tensor_tensor(r8[:, :], mv8[:, :, 1], 1.0 / 1024.0, nm8[:, :], ALU.mult, ALU.add))
        vdr()
        emit("vector", lambda e: e.tensor_scalar(r8[:, :], r8[:, :], 1e-5, None, ALU.add))
        vdr()
        emit("vector", lambda e: e.reciprocal(r8[:, :], r8[:, :]))
        vr = inc("vector")
        wait("scalar", sV, vr)
        emit("scalar", lambda e: e.activation(r8[:, :], r8[:, :], AF.Sqrt, bias=0.0, scale=1.0))
        sr = inc("scalar")
        wait("vector", sC, sr)
        emit("vector", lambda e: e.tensor_tensor(nm8[:, :], mv8[:, :, 0], r8[:, :], ALU.mult))
        vdr()
        emit("vector", lambda e: e.tensor_scalar(nm8[:, :], nm8[:, :], -1.0, None, ALU.mult))
        vdr()
        for a in range(NCH):
            emit("vector", lambda e, a=a: e.tensor_scalar(xn_dst[:, a, :], xsrc[:, a, :], r8[:, a:a + 1], nm8[:, a:a + 1], ALU.mult, ALU.add))
        vdr()

    stA = contextlib.ExitStack()
    sbA = lambda n, s, dt: stA.enter_context(nc.sbuf_tensor(n, s, dt))
    xblk = sbA("xblk", [128, NCH, 1024], F32)
    xnf = sbA("xnf", [128, NCH, 1024], F32)
    xnbk = sbA("xnbk", [128, NCH, 1024], BF16)
    xqbk = sbA("xqbk", [128, NCH, 1024], BF16)
    for i in range([] if os.environ.get("KNOA") else range(NB)) if False else (range(0) if os.environ.get("KNOA") else range(NB)):
        d_x = dma("sync", xblk[:, :, :], x_in[i * TB:(i + 1) * TB, :].rearrange("(a p) d -> p a d", p=128))
        wait("vector", dS, d_x)
        layer_norm_block(xblk, xnf, xqbk[:, 0:1, :].rearrange("p a d -> p (a d)"))
        for a in range(NCH):
            emit("vector", lambda e, a=a: e.tensor_scalar(xnbk[:, a, :], xnf[:, a, :], 1.0, None, ALU.mult))
        vdr()
        for half in range(4):
            vq = quant(xnf[:, 2 * half:2 * half + 2, :].rearrange("p a d -> p (a d)"),
                       xqbk[:, 2 * half:2 * half + 2, :].rearrange("p a d -> p (a d)"), 2048)
        wait("sync", sV, vq)
        d_xn = dma("sync", s_xn[i * TB:(i + 1) * TB, :].rearrange("(a p) d -> p a d", p=128), xnbk[:, :, :])
        d_aw = dma("sync", s_xq[i * TB:(i + 1) * TB, :].rearrange("(a p) d -> p a d", p=128), xqbk[:, :, :])
        wait("vector", dS, d_xn)
        wait("vector", dS, d_aw)
    a_done = cS.v
    barrier()
    stA.close()
    if upto == "A":
        barrier()
        _dbg("xn", s_xn); _dbg("xq", s_xq)
        return _assemble([stQ, st])

    stW = contextlib.ExitStack()
    sbW = lambda n, s, dt: stW.enter_context(nc.sbuf_tensor(n, s, dt))
    wbuf = sbW("wbuf", [128, 2, IF], F32)
    wqb = sbW("wqb", [128, 2, IF], BF16)
    wtb = sbW("wtb", [128, 2, IF], BF16)

    wq_vz = [0, 0]

    def wquant(src_hbm, dst_tm, rows, cols):
        for j, r0 in enumerate(range(0, rows, 128)):
            p = j % 2
            wait("gpsimd", sV, wq_vz[p])
            d_w = dma("gpsimd", wbuf[:, p, 0:cols], src_hbm[r0:r0 + 128, :])
            wait("vector", dG, d_w)
            wait("scalar", dG, d_w)
            vz = quant(wbuf[:, p, 0:cols], wqb[:, p, 0:cols], cols)
            wq_vz[p] = vz
            wait("scalar", sV, vz)
            d_wr = dma("scalar", dst_tm[r0:r0 + 128, :], wqb[:, p, 0:cols])
            if j >= 1:
                wait("vector", dA, d_wr - 16)

    if not skip_w:
        wquant(w_gate, s_wg, IF, D)
        wquant(w_up, s_wu, IF, D)
        wquant(w_down, s_wd, D, IF)
    w_wr = cA.v
    wait("scalar", dA, w_wr)

    def wtrans(src_tm, dst_t, rows, cols):
        for j, c0 in enumerate(range(0, cols, 128)):
            p = j % 2
            d_t = dma("scalar", wtb[:, p, 0:rows], src_tm[:, c0:c0 + 128], transpose=True)
            wait("scalar", dA, d_t)
            dma("scalar", dst_t[c0:c0 + 128, :], wtb[:, p, 0:rows])

    if not skip_w:
        wtrans(s_wg, s_wgt, IF, D)
        wtrans(s_wu, s_wut, IF, D)
        wtrans(s_wd, s_wdt, D, IF)
    w_done = cA.v
    if upto == "W":
        barrier()
        _dbg("wg", s_wg); _dbg("wgt", s_wgt); _dbg("wdt", s_wdt); _dbg("wut", s_wut)
        return _assemble([stW, stQ, st])

    stB = contextlib.ExitStack()
    sbB = lambda n, s, dt: stB.enter_context(nc.sbuf_tensor(n, s, dt))
    psB = lambda n, s: stB.enter_context(nc.psum_tensor(n, s, F32))
    xnfm = sbB("xnfm", [128, NCH, T], BF16)
    bq_row = sbB("bq_row", [1, 2048], BF16)
    d_bqr = dma("gpsimd", bq_row[:1, :], bqkv[0:2048].unsqueeze(0))
    wait("tensor", dG, d_bqr)
    wti = sbB("wti", [128, 2, NCH, 128], BF16)
    wv_all = sbB("wv_all", [128, NCH, D], BF16)
    qkev = sbB("qkev", [128, 2, 512], BF16)
    vtev = sbB("vtev", [128, 2, 1024], BF16)
    pq = [psB(f"pq{i}", [128, 512]) for i in range(6)]
    poolB = Pool(pq)

    wait("sync", dS, a_done)
    for c in range(NCH):
        dma("sync", xnfm[:, c, :], s_xn[:, 128 * c:128 * c + 128], transpose=True)
    b_fm = cS.v
    wait("gpsimd", dS, b_fm)
    d_wv = dma("gpsimd", wv_all[:, :, :], wqkv[:, 2048:3072].rearrange("(c p) n -> p c n", p=128))
    wait("tensor", dS, b_fm)
    wait("tensor", dG, d_smalls)
    if upto == "B1":
        barrier()
        _dbg("xnfm_nop", s_xn)
        return _assemble([stB, stW, stQ, st])

    b_tmm = [0, 0]
    NDC = int(os.environ.get("KNDC", "16"))
    wait("gpsimd", dS, b_fm)
    for dc in range(NDC):
        wait("gpsimd", sT, b_tmm[dc % 2])
        d_wt = dma("gpsimd", wti[:, dc % 2, :, :], wqkv[:, 128 * dc:128 * dc + 128].rearrange("(c p) n -> p c n", p=128))
        wait("tensor", dG, d_wt)
        dstT = s_q0 if dc < 8 else s_k0
        rb = (dc % 8) * 128
        NTB = int(os.environ.get("KNTB", "8"))
        NOBIAS = os.environ.get("KNOBIAS")
        for tb in range(NTB):
            k_, pt = poolB.get("tensor")
            for di in range(NCH):
                emit("tensor", lambda e, di=di, dc=dc, tb=tb, pt=pt, NOBIAS=NOBIAS: e.matmul(
                    pt[:, :], wti[:, dc % 2, di, :], xnfm[:, di, 512 * tb:512 * tb + 512],
                    start=(di == 0), stop=(di == NCH - 1 and bool(NOBIAS))))
            if not NOBIAS:
                emit("tensor", lambda e, dc=dc, pt=pt: e.matmul(
                    pt[:, :], bq_row[:1, 128 * dc:128 * dc + 128], ones_r5[:1, :],
                    start=False, stop=True))
            tmm = inc("tensor")
            wait("scalar", sT, tmm)
            emit("scalar", lambda e, tb=tb, pt=pt: e.activation(
                qkev[:, tb % 2, :], pt[:, :], AF.Copy, bias=0.0, scale=1.0))
            sev = inc("scalar")
            poolB.release(k_, "scalar", sev)
            wait("sync", sC, sev)
            d_o = dma("sync", dstT[rb:rb + 128, 512 * tb:512 * tb + 512], qkev[:, tb % 2, :])
            if tb >= 1:
                wait("scalar", dS, d_o - 16)
        b_tmm[dc % 2] = tmm
    if upto == "B2":
        barrier()
        _dbg("q0", s_q0)
        return _assemble([stB, stW, stQ, st])
    wait("tensor", dG, d_wv)
    for tc in range(32):
        kk = []
        for half in range(2):
            k_, pt = poolB.get("tensor")
            kk.append((k_, pt))
            for di in range(NCH):
                emit("tensor", lambda e, di=di, tc=tc, half=half, pt=pt: e.matmul(
                    pt[:, :], xnfm[:, di, 128 * tc:128 * tc + 128], wv_all[:, di, 512 * half:512 * half + 512],
                    start=(di == 0), stop=False))
            emit("tensor", lambda e, half=half, pt=pt: e.matmul(
                pt[:, :], ones_r[:1, :], bv_row[:1, 512 * half:512 * half + 512], start=False, stop=True))
        tmm = inc("tensor")
        wait("scalar", sT, tmm)
        for half in range(2):
            k_, pt = kk[half]
            emit("scalar", lambda e, half=half, tc=tc, pt=pt: e.activation(
                vtev[:, tc % 2, 512 * half:512 * half + 512], pt[:, :], AF.Copy, bias=0.0, scale=1.0))
            sev = inc("scalar")
            poolB.release(k_, "scalar", sev)
        wait("sync", sC, sev)
        d_o = dma("sync", s_v0[128 * tc:128 * tc + 128, :], vtev[:, tc % 2, :])
        if tc >= 1:
            wait("scalar", dS, d_o - 16)
    barrier()
    stB.close()
    if upto == "B":
        _dbg("q0", s_q0); _dbg("k0", s_k0); _dbg("v0", s_v0)
        return _assemble([stW, stQ, st])

    stCD = contextlib.ExitStack()
    kv_all = stCD.enter_context(nc.sbuf_tensor("kv_all", [128, H * 132], BF16))
    stC = contextlib.ExitStack()
    sbC = lambda n, s, dt: stC.enter_context(nc.sbuf_tensor(n, s, dt))
    psC = lambda n, s: stC.enter_context(nc.psum_tensor(n, s, F32))
    vf2 = sbC("vf2", [128, 32, H * 66], BF16)
    khd = sbC("khd", [64, T], BF16)
    ksqd = sbC("ksqd", [64, T], BF16)
    kpr = sbC("kpr", [128, 32, 256], BF16)
    bcolC = sbC("bcolC", [128, 1], F32)
    pk = [psC(f"pk{i}", [128, 257]) for i in range(2)]
    pkv = psC("pkv", [128, 132])
    poolC = Pool(pk)

    emit("vector", lambda e: e.memset(vf2[:, :, :].rearrange("p a (h c) -> p a h c", c=66)[:, :, :, 64:66], 1.0))
    v_vf = inc("vector")
    for tc in range(32):
        dma("sync", vf2[:, tc, :].rearrange("p (h c) -> p h c", c=66)[:, :, 0:64],
            s_v0[128 * tc:128 * tc + 128, :].rearrange("p (h c) -> p h c", c=64))
    c_vf = cS.v
    wait("tensor", dS, c_vf)
    wait("tensor", sV, v_vf)

    prev_tC = [0]
    for h in range(H):
        d_k = dma("sync", khd[:, :], s_k0[64 * h:64 * h + 64, :])
        wait("scalar", dS, d_k)
        wait("scalar", sT, prev_tC[0])
        emit("scalar", lambda e: e.activation(ksqd[:, :], khd[:, :], AF.Square, bias=0.0, scale=1.0))
        s_ksq = inc("scalar")
        wait("tensor", sC, s_ksq)
        for tc in range(32):
            k_, pt = poolC.get("tensor")
            emit("tensor", lambda e, tc=tc, pt=pt: e.matmul(
                pt[:, :], khd[:, 128 * tc:128 * tc + 128], rhk_o[:, :], start=True, stop=False))
            emit("tensor", lambda e, tc=tc, pt=pt: e.matmul(
                pt[:, :], ksqd[:, 128 * tc:128 * tc + 128], rhk_s[:, :], start=False, stop=True))
            tmm = inc("tensor")
            wait("scalar", sT, tmm)
            emit("scalar", lambda e, pt=pt: e.activation(bcolC[:, :], pt[:, 256:257], AF.Copy, bias=0.0, scale=1.0))
            sdr()
            emit("scalar", lambda e, tc=tc, pt=pt: e.activation(
                kpr[:, tc, :], pt[:, 0:256], AF.Exp, bias=bcolC[:, :], scale=1.0))
            sev = inc("scalar")
            poolC.release(k_, "scalar", sev)
        wait("tensor", sC, cC.v)
        for tc in range(32):
            for mc in range(2):
                emit("tensor", lambda e, tc=tc, mc=mc, h=h: e.matmul(
                    pkv[:, 66 * mc:66 * mc + 66], kpr[:, tc, 128 * mc:128 * mc + 128],
                    vf2[:, tc, 66 * h:66 * h + 66], start=(tc == 0), stop=(tc == 31),
                    skip_group_check=True))
        t_kv = inc("tensor")
        wait("scalar", sT, t_kv)
        emit("scalar", lambda e, h=h: e.activation(kv_all[:, 132 * h:132 * h + 132], pkv[:, :], AF.Copy, bias=0.0, scale=1.0))
        v_kv = inc("scalar")
        wait("tensor", sC, v_kv)
        prev_tC[0] = t_kv
        wait("sync", sT, t_kv)
    barrier()
    stC.close()
    if upto == "C":
        kvd = ap(nc.dram_tensor("dbg_kv_d", [128, H * 132], BF16, kind="Internal"))
        dma("sync", kvd.rearrange("p f -> p f") if False else kvd, kv_all[:, :])
        kprd = ap(nc.dram_tensor("dbg_kpr_d", [128, 32 * 256], BF16, kind="Internal"))
        dma("sync", kprd.rearrange("p (a m) -> p a m", m=256), kpr[:, :, :])
        khdd = ap(nc.dram_tensor("dbg_khd_d", [64, T], BF16, kind="Internal"))
        dma("sync", khdd, khd[:, :])
        ksqd2 = ap(nc.dram_tensor("dbg_ksq_d", [64, T], BF16, kind="Internal"))
        dma("sync", ksqd2, ksqd[:, :])
        wait("sync", dS, cS.v)
        _dbg("kv", kvd)
        _dbg("kpr", kprd)
        _dbg("khd", khdd)
        _dbg("ksq", ksqd2)
        return _assemble([stC, stCD, stW, stQ, st])

    stD = contextlib.ExitStack()
    sbD = lambda n, s, dt: stD.enter_context(nc.sbuf_tensor(n, s, dt))
    psD = lambda n, s: stD.enter_context(nc.psum_tensor(n, s, F32))
    qhd = sbD("qhd", [64, T], BF16)
    qsqd = sbD("qsqd", [64, T], BF16)
    qpr = sbD("qpr", [128, 2, T], BF16)
    o_sb = sbD("o_sb", [128, 32, 1024], BF16)
    rclD = sbD("rclD", [128, 1], F32)
    numt = sbD("numt", [128, 2, 66], F32)
    pqp = [psD(f"pqp{i}", [128, 512]) for i in range(4)]
    pnm = [psD(f"pnm{i}", [128, 66]) for i in range(2)]
    poolD = Pool(pqp)
    poolN = Pool(pnm)

    prev_tD = [0]
    for h in range(H):
        d_q = dma("sync", qhd[:, :], s_q0[64 * h:64 * h + 64, :])
        wait("scalar", dS, d_q)
        wait("scalar", sT, prev_tD[0])
        emit("scalar", lambda e: e.activation(qsqd[:, :], qhd[:, :], AF.Square, bias=0.0, scale=1.0))
        s_qsq = inc("scalar")
        wait("tensor", sC, s_qsq)
        for mc in range(2):
            for tb in range(8):
                k_, pt = poolD.get("tensor")
                emit("tensor", lambda e, mc=mc, tb=tb, pt=pt: e.matmul(
                    pt[:, :], omx_o[:, mc, :], qhd[:, 512 * tb:512 * tb + 512], start=True, stop=False))
                emit("tensor", lambda e, mc=mc, tb=tb, pt=pt: e.matmul(
                    pt[:, :], negh[:, :], qsqd[:, 512 * tb:512 * tb + 512], start=False, stop=True))
                tmm = inc("tensor")
                wait("scalar", sT, tmm)
                emit("scalar", lambda e, mc=mc, tb=tb, pt=pt: e.activation(
                    qpr[:, mc, 512 * tb:512 * tb + 512], pt[:, :], AF.Exp, bias=0.0, scale=1.0))
                sev = inc("scalar")
                poolD.release(k_, "scalar", sev)
        wait("tensor", sC, cC.v)
        numt_v = [0, 0]
        for tc in range(32):
            k_, pn = poolN.get("tensor")
            for mc in range(2):
                emit("tensor", lambda e, tc=tc, mc=mc, h=h, pn=pn: e.matmul(
                    pn[:, :], qpr[:, mc, 128 * tc:128 * tc + 128],
                    kv_all[:, 132 * h + 66 * mc:132 * h + 66 * mc + 66],
                    start=(mc == 0), stop=(mc == 1)))
            tmm = inc("tensor")
            wait("scalar", sT, tmm)
            wait("scalar", sV, numt_v[tc % 2])
            emit("scalar", lambda e, tc=tc, pn=pn: e.activation(numt[:, tc % 2, :], pn[:, :], AF.Copy, bias=0.0, scale=1.0))
            sevn = inc("scalar")
            poolN.release(k_, "scalar", sevn)
            wait("vector", sC, sevn)
            emit("vector", lambda e, tc=tc: e.tensor_scalar(rclD[:, :], numt[:, tc % 2, 64:65], 1e-6, None, ALU.add))
            vdr()
            emit("vector", lambda e: e.reciprocal(rclD[:, :], rclD[:, :]))
            vdr()
            if os.environ.get("KDIAG") == "num":
                emit("vector", lambda e, tc=tc, h=h: e.tensor_scalar(
                    o_sb[:, tc, 64 * h:64 * h + 64], numt[:, tc % 2, 0:64], 1.0, None, ALU.mult))
            elif os.environ.get("KDIAG") == "den":
                emit("vector", lambda e, tc=tc, h=h: e.tensor_scalar(
                    o_sb[:, tc, 64 * h:64 * h + 64], numt[:, tc % 2, 64:65].broadcast_to([128, 64]), 1.0, None, ALU.mult))
            else:
                emit("vector", lambda e, tc=tc, h=h: e.tensor_scalar(
                    o_sb[:, tc, 64 * h:64 * h + 64], numt[:, tc % 2, 0:64], rclD[:, :], None, ALU.mult))
            vev = inc("vector")
            numt_v[tc % 2] = vev
        prev_tD[0] = cT.v
        wait("sync", sV, cV.v)
        wait("sync", sT, cT.v)
    v_oD = cV.v
    wait("sync", sV, v_oD)
    for tc in range(32):
        dma("sync", s_o0[128 * tc:128 * tc + 128, :], o_sb[:, tc, :])
    barrier()
    if upto == "D":
        qprd = ap(nc.dram_tensor("dbg_qpr_d", [128, 2 * T], BF16, kind="Internal"))
        dma("sync", qprd.rearrange("p (a m) -> p a m", m=T), qpr[:, :, :])
        osbd = ap(nc.dram_tensor("dbg_osb_d", [128, 32 * 1024], BF16, kind="Internal"))
        dma("sync", osbd.rearrange("p (a m) -> p a m", m=1024), o_sb[:, :, :])
        wait("sync", dS, cS.v)
        _dbg("o0", s_o0)
        _dbg("qpr", qprd)
        _dbg("osb", osbd)
        return _assemble([stD, stCD, stW, stQ, st])
    stD.close()
    stCD.close()

    stW.close()
    stE = contextlib.ExitStack()
    sbE = lambda n, s, dt: stE.enter_context(nc.sbuf_tensor(n, s, dt))
    psE = lambda n, s: stE.enter_context(nc.psum_tensor(n, s, F32))
    wo_sb = sbE("wo_sb", [128, NCH, D], BF16)
    ebuf = sbE("ebuf", [128, 8192], BF16)
    xqfm = sbE("xqfm", [128, NCH, TB], BF16)
    xeb = sbE("xeb", [128, NCH, 1024], F32)
    xn1b = sbE("xn1b", [128, NCH, 1024], BF16)
    hbl = sbE("hbl", [128, NCH * IF], BF16)
    wdn = sbE("wdn", [128, 1, 1024], BF16)
    evt = sbE("evt", [128, 2, 512], F32)
    pe = [psE(f"pe{i}", [128, 512]) for i in range(8)]
    poolE = Pool(pe)
    ofm = ebuf[:, :].rearrange("p (a t) -> p a t", t=1024)
    wgu = ebuf[:, :].rearrange("p (g a n) -> p g a n", g=2, a=NCH)
    hblA = hbl[:, :].rearrange("p (a n) -> p a n", a=NCH)
    hblB = hbl[:, :].rearrange("p (a t) -> p a t", t=1024)

    d_wo = dma("gpsimd", wo_sb[:, :, :], wout.rearrange("(c p) n -> p c n", p=128))
    NWIN = [(0, 512), (512, 512), (1024, 512), (1536, 512), (2048, 512), (2560, 256)]

    for i in range(NB):
        own_blk = i < 2
        for c in range(NCH):
            dma("sync", ofm[:, c, :], s_o0[i * TB:(i + 1) * TB, 128 * c:128 * c + 128], transpose=True)
        d_x = dma("sync", xeb[:, :, :], x_in[i * TB:(i + 1) * TB, :].rearrange("(a p) d -> p a d", p=128))
        wait("tensor", dS, d_x)
        wait("tensor", dG, d_wo)
        for tc in range(NCH):
            kk = []
            for half in range(2):
                k_, pt = poolE.get("tensor")
                kk.append((k_, pt))
                for di in range(NCH):
                    emit("tensor", lambda e, di=di, tc=tc, half=half, pt=pt: e.matmul(
                        pt[:, :], ofm[:, di, 128 * tc:128 * tc + 128], wo_sb[:, di, 512 * half:512 * half + 512],
                        start=(di == 0), stop=False))
                emit("tensor", lambda e, half=half, pt=pt: e.matmul(
                    pt[:, :], ones_r[:1, :], bo_row[:1, 512 * half:512 * half + 512], start=False, stop=True))
            tmm = inc("tensor")
            wait("scalar", sT, tmm)
            for half in range(2):
                k_, pt = kk[half]
                emit("scalar", lambda e, tc=tc, half=half, pt=pt: e.activation(
                    evt[:, half, :], pt[:, :], AF.Copy, bias=0.0, scale=1.0))
                sev = inc("scalar")
                poolE.release(k_, "scalar", sev)
            wait("vector", sC, sev)
            for half in range(2):
                emit("vector", lambda e, tc=tc, half=half: e.tensor_tensor(
                    xeb[:, tc, 512 * half:512 * half + 512], xeb[:, tc, 512 * half:512 * half + 512], evt[:, half, :], ALU.add))
            vev = inc("vector")
            wait("scalar", sV, vev)
        v_y = inc("vector")
        wait("scalar", dA, w_done)
        for c in range(NCH):
            dma("sync", xqfm[:, c, :], s_xq[i * TB:(i + 1) * TB, 128 * c:128 * c + 128], transpose=True)
        e_xq = cS.v
        wait("tensor", dS, e_xq)
        wait("scalar", sV, v_y)
        e_guT = [0]
        for nbi, (n0, nw) in enumerate(NWIN):
            wait("scalar", sT, e_guT[0])
            d_g = dma("scalar", wgu[:, 0, :, 0:nw], s_wgt[:, n0:n0 + nw].rearrange("(c p) n -> p c n", p=128))
            d_u = dma("scalar", wgu[:, 1, :, 0:nw], s_wut[:, n0:n0 + nw].rearrange("(c p) n -> p c n", p=128))
            wait("tensor", dA, d_u)
            for tc in range(NCH):
                kk = []
                for gu in range(2):
                    k_, pt = poolE.get("tensor")
                    kk.append((k_, pt))
                    for di in range(NCH):
                        emit("tensor", lambda e, di=di, tc=tc, gu=gu, nw=nw, pt=pt: e.matmul(
                            pt[:, 0:nw], xqfm[:, di, 128 * tc:128 * tc + 128], wgu[:, gu, di, 0:nw],
                            start=(di == 0), stop=(di == NCH - 1)))
                tmm = inc("tensor")
                (kg, pg), (ku, pu) = kk
                wait("scalar", sT, tmm)
                emit("scalar", lambda e, nw=nw, pg=pg: e.activation(qt_t2[:, 0:nw], pg[:, 0:nw], AF.Silu, bias=0.0, scale=1.0))
                sdr()
                emit("scalar", lambda e, nw=nw, pu=pu: e.activation(evt[:, 0, 0:nw], pu[:, 0:nw], AF.Copy, bias=0.0, scale=1.0))
                sev = inc("scalar")
                poolE.release(kg, "scalar", sev)
                poolE.release(ku, "scalar", sev)
                wait("vector", sC, sev)
                emit("vector", lambda e, tc=tc, n0=n0, nw=nw: e.tensor_tensor(
                    hblA[:, tc, n0:n0 + nw], qt_t2[:, 0:nw], evt[:, 0, 0:nw], ALU.mult))
                vev = inc("vector")
                wait("scalar", sV, vev)
            e_guT[0] = tmm
        for tc in range(NCH):
            vz = quant(hblA[:, tc, :], hblA[:, tc, :], IF)
            wait("sync", sV, vz)
            dma("sync", s_hq[128 * tc:128 * tc + 128, :], hblA[:, tc, :])
        e_hw = cS.v
        wait("sync", dS, e_hw)
        for ncix in range(22):
            dma("sync", hblB[:, ncix, :], s_hq[:, 128 * ncix:128 * ncix + 128], transpose=True)
        e_hr = cS.v
        wait("tensor", dS, e_hr)
        for grp in range(2):
            banks = []
            for tcl in range(4):
                for half in range(2):
                    k_, pt = poolE.get("tensor")
                    banks.append((k_, pt))
            for ncix in range(22):
                d_d = dma("scalar", wdn[:, 0, :], s_wdt[128 * ncix:128 * ncix + 128, :])
                wait("tensor", dA, d_d)
                for tcl in range(4):
                    tc = 4 * grp + tcl
                    for half in range(2):
                        pt = banks[2 * tcl + half][1]
                        emit("tensor", lambda e, ncix=ncix, tc=tc, half=half, pt=pt: e.matmul(
                            pt[:, :], hblB[:, ncix, 128 * (tc % 8):128 * (tc % 8) + 128],
                            wdn[:, 0, 512 * half:512 * half + 512],
                            start=(ncix == 0), stop=(ncix == 21), skip_group_check=True))
                t_nd = inc("tensor")
                if ncix < 21:
                    wait("scalar", sT, t_nd)
            t_dn = inc("tensor")
            wait("scalar", sT, t_dn)
            for tcl in range(4):
                tc = 4 * grp + tcl
                for half in range(2):
                    k_, pt = banks[2 * tcl + half]
                    emit("scalar", lambda e, half=half, pt=pt: e.activation(
                        evt[:, half, :], pt[:, :], AF.Copy, bias=0.0, scale=1.0))
                    sev = inc("scalar")
                    poolE.release(k_, "scalar", sev)
                wait("vector", sC, sev)
                for half in range(2):
                    emit("vector", lambda e, tc=tc, half=half: e.tensor_tensor(
                        xeb[:, tc, 512 * half:512 * half + 512], xeb[:, tc, 512 * half:512 * half + 512],
                        evt[:, half, :], ALU.add))
                vev = inc("vector")
                wait("scalar", sV, vev)
        layer_norm_block(xeb, xn1b, qt_a[:, 0:1024])
        v_ln1 = inc("vector")
        wait("sync", sV, v_ln1)
        dma("sync", s_xn1[i * TB:(i + 1) * TB, :].rearrange("(a p) d -> p a d", p=128), xn1b[:, :, :])
        if own_blk:
            io = i % 2
            dma("sync", s_x1[io * TB:(io + 1) * TB, :].rearrange("(a p) d -> p a d", p=128), xeb[:, :, :])
        e_st = cS.v
        wait("vector", dS, e_st)
    barrier()
    stE.close()
    stQ.close()
    if upto == "E":
        _dbg("x1", s_x1); _dbg("xn1", s_xn1)
        return _assemble([st])

    stF = contextlib.ExitStack()
    sbF = lambda n, s, dt: stF.enter_context(nc.sbuf_tensor(n, s, dt))
    q1 = st.enter_context(nc.sbuf_tensor("q1", [128, NCH, TO], BF16))
    psF = lambda n, s: stF.enter_context(nc.psum_tensor(n, s, F32))
    wqt = sbF("wqt", [128, NCH, D], BF16)
    wkt = sbF("wkt", [128, NCH, D], BF16)
    wvt = sbF("wvt", [128, NCH, D], BF16)
    xn1f = sbF("xn1f", [128, NCH, TB], BF16)
    kv1ev = sbF("kv1ev", [128, 2, 1024], BF16)
    pf = [psF(f"pf{i}", [128, 512]) for i in range(6)]
    poolF = Pool(pf)

    for di in range(NCH):
        dma("sync", wqt[:, di, :], s_wq1[:, 128 * di:128 * di + 128], transpose=True)
        dma("sync", wkt[:, di, :], s_wk1[:, 128 * di:128 * di + 128], transpose=True)
        dma("sync", wvt[:, di, :], s_wv1[:, 128 * di:128 * di + 128], transpose=True)
    f_w = cS.v
    wait("tensor", dS, f_w)
    for i in range(NB):
        own_blk = i < 2
        for c in range(NCH):
            dma("sync", xn1f[:, c, :], s_xn1[i * TB:(i + 1) * TB, 128 * c:128 * c + 128], transpose=True)
        f_x = cS.v
        wait("tensor", dS, f_x)
        wlist = [("k", wkt, s_k1)] + ([("q", wqt, None)] if own_blk else [])
        for nm, wt, dstT in wlist:
            for dc in range(NCH):
                for tb in range(2):
                    k_, pt = poolF.get("tensor")
                    for di in range(NCH):
                        emit("tensor", lambda e, di=di, dc=dc, tb=tb, wt=wt, pt=pt: e.matmul(
                            pt[:, :], wt[:, di, 128 * dc:128 * dc + 128], xn1f[:, di, 512 * tb:512 * tb + 512],
                            start=(di == 0), stop=(di == NCH - 1)))
                    tmm = inc("tensor")
                    wait("scalar", sT, tmm)
                    if nm == "k":
                        emit("scalar", lambda e, tb=tb, pt=pt: e.activation(
                            kv1ev[:, tb % 2, 0:512], pt[:, :], AF.Copy, bias=0.0, scale=1.0))
                        sev = inc("scalar")
                        poolF.release(k_, "scalar", sev)
                        wait("sync", sC, sev)
                        d_o = dma("sync", s_k1[128 * dc:128 * dc + 128, i * TB + 512 * tb:i * TB + 512 * tb + 512],
                                  kv1ev[:, tb % 2, 0:512])
                        wait("scalar", dS, d_o - 16)
                    else:
                        io = i % 2
                        emit("scalar", lambda e, dc=dc, io=io, tb=tb, pt=pt: e.activation(
                            q1[:, dc, io * TB + 512 * tb:io * TB + 512 * tb + 512], pt[:, :], AF.Copy, bias=0.0, scale=1.0))
                        sev = inc("scalar")
                        poolF.release(k_, "scalar", sev)
        wait("scalar", dS, cS.v)
        for tc in range(NCH):
            kk = []
            for half in range(2):
                k_, pt = poolF.get("tensor")
                kk.append((k_, pt))
                for di in range(NCH):
                    emit("tensor", lambda e, di=di, tc=tc, half=half, pt=pt: e.matmul(
                        pt[:, :], xn1f[:, di, 128 * tc:128 * tc + 128], wvt[:, di, 512 * half:512 * half + 512],
                        start=(di == 0), stop=(di == NCH - 1)))
            tmm = inc("tensor")
            wait("scalar", sT, tmm)
            for half in range(2):
                k_, pt = kk[half]
                emit("scalar", lambda e, tc=tc, half=half, pt=pt: e.activation(
                    kv1ev[:, tc % 2, 512 * half:512 * half + 512], pt[:, :], AF.Copy, bias=0.0, scale=1.0))
                sev = inc("scalar")
                poolF.release(k_, "scalar", sev)
            wait("sync", sC, sev)
            d_o = dma("sync", s_v1[i * TB + 128 * tc:i * TB + 128 * tc + 128, :], kv1ev[:, tc % 2, :])
            if tc >= 1:
                wait("scalar", dS, d_o - 16)
    barrier()
    stF.close()
    if upto == "F":
        _dbg("k1", s_k1); _dbg("v1", s_v1)
        return _assemble([st])

    stF2 = contextlib.ExitStack()
    sbF2 = lambda n, s, dt: stF2.enter_context(nc.sbuf_tensor(n, s, dt))
    x1p = st.enter_context(nc.sbuf_tensor("x1p", [128, 16, 1024], F32))
    psF2 = lambda n, s: stF2.enter_context(nc.psum_tensor(n, s, F32))
    wct = sbF2("wct", [128, NCH, D], BF16)
    xn1f2 = sbF2("xn1f2", [128, NCH, TB], BF16)
    sm_ld = sbF2("sm_ld", [4, 100], F32)
    sm_lu = sbF2("sm_lu", [50, 4], F32)
    sm_da = sbF2("sm_da", [100, D], F32)
    sm_ua = sbF2("sm_ua", [128, NCH, 50], F32)
    sm_ldT = sbF2("sm_ldT", [100, 4], F32)
    sm_luT = sbF2("sm_luT", [4, 50], F32)
    sm_uaT = sbF2("sm_uaT", [50, NCH * 128], F32)
    sm_A = sbF2("sm_A", [4, D], F32)
    sm_B = sbF2("sm_B", [50, D], F32)
    lilc = sbF2("lilc", [128, D], BF16)
    wtch = sbF2("wtch", [128, D], BF16)
    evt2 = sbF2("evt2", [128, 2, 512], F32)
    pf2 = [psF2(f"pf2{i}", [128, 512]) for i in range(4)]
    pS = psF2("pS", [128, 128], )
    poolF2 = Pool(pf2)

    d1 = dma("sync", sm_ld[:, :], l_dn)
    d2 = dma("sync", sm_lu[:, :], l_up)
    d3 = dma("sync", sm_da[:, :], d_aux)
    d4 = dma("sync", sm_ua[:, :, :], u_aux.rearrange("(c p) u -> p c u", p=128))
    wait("tensor", dS, d4)
    emit("tensor", lambda e: e.transpose(pS[0:100, 0:4], sm_ld[:, :], identf[0:4, 0:4]))
    t1 = inc("tensor")
    wait("scalar", sT, t1)
    emit("scalar", lambda e: e.activation(sm_ldT[:, :], pS[0:100, 0:4], AF.Copy, bias=0.0, scale=1.0))
    v1 = inc("scalar")
    wait("tensor", sC, v1)
    emit("tensor", lambda e: e.transpose(pS[0:4, 0:50], sm_lu[:, :], identf[0:50, 0:50]))
    t2 = inc("tensor")
    wait("scalar", sT, t2)
    emit("scalar", lambda e: e.activation(sm_luT[:, :], pS[0:4, 0:50], AF.Copy, bias=0.0, scale=1.0))
    v2 = inc("scalar")
    wait("tensor", sC, v2)
    for c in range(NCH):
        emit("tensor", lambda e, c=c: e.transpose(pS[0:50, 0:128], sm_ua[:, c, :], identf[:, :]))
        t3 = inc("tensor")
        wait("scalar", sT, t3)
        emit("scalar", lambda e, c=c: e.activation(sm_uaT[:, 128 * c:128 * c + 128], pS[0:50, 0:128], AF.Copy, bias=0.0, scale=1.0))
        v3 = inc("scalar")
        wait("tensor", sC, v3)
    for half in range(2):
        emit("tensor", lambda e, half=half: e.matmul(pf2[0][0:4, :], sm_ldT[:, :], sm_da[:, 512 * half:512 * half + 512], start=True, stop=True))
        t4 = inc("tensor")
        wait("scalar", sT, t4)
        emit("scalar", lambda e, half=half: e.activation(sm_A[:, 512 * half:512 * half + 512], pf2[0][0:4, 0:512], AF.Copy, bias=0.0, scale=1.0))
        v4 = inc("scalar")
        wait("tensor", sC, v4)
    for half in range(2):
        emit("tensor", lambda e, half=half: e.matmul(pf2[0][0:50, :], sm_luT[:, :], sm_A[:, 512 * half:512 * half + 512], start=True, stop=True))
        t5 = inc("tensor")
        wait("scalar", sT, t5)
        emit("scalar", lambda e, half=half: e.activation(sm_B[:, 512 * half:512 * half + 512], pf2[0][0:50, 0:512], AF.Copy, bias=0.0, scale=1.0))
        v5 = inc("scalar")
        wait("tensor", sC, v5)
    for ic in range(NCH):
        d_wt = dma("sync", wtch[:, :], s_w1[:, 128 * ic:128 * ic + 128], transpose=True)
        for half in range(2):
            emit("tensor", lambda e, ic=ic, half=half: e.matmul(
                pf2[1][:, :], sm_B[:, 128 * ic:128 * ic + 128], sm_uaT[:, 512 * half:512 * half + 512],
                start=True, stop=True))
            t6 = inc("tensor")
            wait("scalar", sT, t6)
            emit("scalar", lambda e, half=half: e.activation(lilc[:, 512 * half:512 * half + 512], pf2[1][:, :], AF.Copy, bias=0.0, scale=1.0))
            s6 = inc("scalar")
            wait("tensor", sC, s6)
        s6b = cC.v
        wait("vector", sC, s6b)
        wait("vector", dS, d_wt)
        emit("vector", lambda e, ic=ic: e.tensor_tensor(wct[:, ic, :], lilc[:, :], wtch[:, :], ALU.add))
        v6 = inc("vector")
        wait("sync", sV, v6)
    d_x1 = dma("sync", x1p[:, :, :], s_x1[:, :].rearrange("(a p) d -> p a d", p=128))
    prev_f2t = None
    for ib in range(2):
        if prev_f2t is not None:
            wait("sync", sT, prev_f2t)
        for c in range(NCH):
            dma("sync", xn1f2[:, c, :], s_xn1[ib * TB:(ib + 1) * TB, 128 * c:128 * c + 128], transpose=True)
        f2_x = cS.v
        wait("tensor", dS, f2_x)
        wait("tensor", dS, d_x1)
        for tc in range(NCH):
            kk = []
            for half in range(2):
                k_, pt = poolF2.get("tensor")
                kk.append((k_, pt))
                for di in range(NCH):
                    emit("tensor", lambda e, di=di, tc=tc, half=half, pt=pt: e.matmul(
                        pt[:, :], xn1f2[:, di, 128 * tc:128 * tc + 128], wct[:, di, 512 * half:512 * half + 512],
                        start=(di == 0), stop=False))
                emit("tensor", lambda e, half=half, pt=pt: e.matmul(
                    pt[:, :], ones_r[:1, :], b1_row[:1, 512 * half:512 * half + 512], start=False, stop=True))
            tmm = inc("tensor")
            wait("scalar", sT, tmm)
            for half in range(2):
                k_, pt = kk[half]
                emit("scalar", lambda e, half=half, pt=pt: e.activation(
                    evt2[:, half, :], pt[:, :], AF.Copy, bias=0.0, scale=1.0))
                sev = inc("scalar")
                poolF2.release(k_, "scalar", sev)
            wait("vector", sC, sev)
            for half in range(2):
                emit("vector", lambda e, ib=ib, tc=tc, half=half: e.tensor_tensor(
                    x1p[:, 8 * ib + tc, 512 * half:512 * half + 512],
                    x1p[:, 8 * ib + tc, 512 * half:512 * half + 512], evt2[:, half, :], ALU.add))
            vev = inc("vector")
            wait("scalar", sV, vev)
        prev_f2t = tmm
    barrier()
    stF2.close()
    if upto == "F2":
        x1pd = ap(nc.dram_tensor("dbg_x1p_d", [TO, D], F32, kind="Internal"))
        dma("sync", x1pd.rearrange("(a p) d -> p a d", p=128), x1p[:, :, :])
        wait("sync", dS, cS.v)
        _dbg("x1p", x1pd)
        return _assemble([st])

    stG = contextlib.ExitStack()
    sbG = lambda n, s, dt: stG.enter_context(nc.sbuf_tensor(n, s, dt))
    psG = lambda n, s: stG.enter_context(nc.psum_tensor(n, s, F32))
    PT = sbG("PT", [128, 32, 512], BF16)
    kst = sbG("kst", [128, 2, NCH, 128], BF16)
    vst = sbG("vst", [128, 2, 1024], BF16)
    drow = sbG("drow", [1, 512], F32)
    rc4 = sbG("rc4", [128, 4], F32)
    rc4t = sbG("rc4t", [128, 4], F32)
    pvt = sbG("pvt", [128, 2, 512], F32)
    pst = [psG(f"pst{i}", [128, 512]) for i in range(2)]
    pden2 = psG("pden2", [128, 512])
    ppv = [psG(f"ppv{i}", [128, 512]) for i in range(5)]
    poolS = Pool(pst)
    INV = 1.0 / 32.0

    for tq in range(4):
        q0 = 512 * tq
        g_kT = [0, 0]
        for tk in range(32):
            wait("sync", sT, g_kT[tk % 2])
            d_k = dma("sync", kst[:, tk % 2, :, :], s_k1[:, 128 * tk:128 * tk + 128].rearrange("(c p) t -> p c t", p=128))
            wait("tensor", dS, d_k)
            k_, pt = poolS.get("tensor")
            for di in range(NCH):
                emit("tensor", lambda e, di=di, tk=tk, q0=q0, pt=pt: e.matmul(
                    pt[:, :], kst[:, tk % 2, di, :], q1[:, di, q0:q0 + 512],
                    start=(di == 0), stop=(di == NCH - 1)))
            tmm = inc("tensor")
            wait("scalar", sT, tmm)
            emit("scalar", lambda e, tk=tk, pt=pt: e.activation(PT[:, tk, :], pt[:, :], AF.Exp, bias=0.0, scale=INV))
            sev = inc("scalar")
            poolS.release(k_, "scalar", sev)
            g_kT[tk % 2] = tmm
            wait("tensor", sC, sev)
            emit("tensor", lambda e, tk=tk: e.matmul(
                pden2[0:1, :], ones_c[:, :], PT[:, tk, :], start=(tk == 0), stop=(tk == 31),
                skip_group_check=True))
        t_den = inc("tensor")
        wait("scalar", sT, t_den)
        emit("scalar", lambda e: e.activation(drow[:1, :], pden2[0:1, :], AF.Copy, bias=0.0, scale=1.0))
        s_dr = inc("scalar")
        wait("tensor", sC, s_dr)
        for j in range(4):
            emit("tensor", lambda e, j=j: e.transpose(pden2[:, j:j + 1], drow[:1, 128 * j:128 * j + 128], identf[0:1, 0:1]))
        t_dc = inc("tensor")
        wait("scalar", sT, t_dc)
        emit("scalar", lambda e: e.activation(rc4t[:, :], pden2[:, 0:4], AF.Copy, bias=0.0, scale=1.0))
        s_rc = inc("scalar")
        wait("vector", sC, s_rc)
        emit("vector", lambda e: e.reciprocal(rc4[:, :], rc4t[:, :]))
        vdr()
        v_rc = inc("vector")
        pvb = ppv + pst + [pden2]
        g_vT = [0, 0]
        for tk in range(32):
            wait("sync", sT, g_vT[tk % 2])
            d_v = dma("sync", vst[:, tk % 2, :], s_v1[128 * tk:128 * tk + 128, :])
            wait("tensor", dS, d_v)
            if tk == 0:
                wait("tensor", sV, v_rc)
            for tqs in range(4):
                for half in range(2):
                    pt = pvb[2 * tqs + half]
                    emit("tensor", lambda e, tk=tk, tqs=tqs, half=half, pt=pt: e.matmul(
                        pt[:, :], PT[:, tk, 128 * tqs:128 * tqs + 128], vst[:, tk % 2, 512 * half:512 * half + 512],
                        start=(tk == 0), stop=(tk == 31), skip_group_check=True))
            t_pv = inc("tensor")
            g_vT[tk % 2] = t_pv
        t_pv = inc("tensor")
        wait("scalar", sT, t_pv)
        wait("vector", sT, t_pv)
        pvt_v = [0, 0]
        for tqs in range(4):
            for half in range(2):
                pt = pvb[2 * tqs + half]
                xc = (q0 + 128 * tqs) // 128
                wait("scalar", sV, pvt_v[half])
                emit("scalar", lambda e, half=half, pt=pt: e.activation(
                    pvt[:, half, :], pt[:, :], AF.Copy, bias=0.0, scale=1.0))
                s_pv = inc("scalar")
                wait("vector", sC, s_pv)
                emit("vector", lambda e, xc=xc, tqs=tqs, half=half: e.scalar_tensor_tensor(
                    x1p[:, xc, 512 * half:512 * half + 512], pvt[:, half, :], rc4[:, tqs:tqs + 1],
                    x1p[:, xc, 512 * half:512 * half + 512], ALU.mult, ALU.add))
                vdr()
                pvt_v[half] = inc("vector")
        v_pv = cV.v
        wait("tensor", sV, v_pv)
        wait("scalar", sV, v_pv)
    v_fin = cV.v
    wait("sync", sV, v_fin)
    nout = 0
    for c in range(16):
        emit("sync", lambda e, c=c: e.dma_start(out=y_out[128 * c:128 * c + 128, :], in_=x1p[:, c, :]).then_inc(dOUT, 16))
        nout += 16
    wait("sync", dOUT, nout)
    wait("scalar", dOUT, nout)
    stG.close()

    return _assemble([st])


_cache = {}


def _get_program():
    if "nc" not in _cache:
        _cache["nc"] = build_program()
    return _cache["nc"]


def kernel(L0_attn_x, L0_attn_Wqkv, L0_attn_bqkv, L0_attn_Wout, L0_attn_bout,
           L0_attn_omega, L0_ffn_w_gate, L0_ffn_w_up, L0_ffn_w_down,
           L1_attn_Wq, L1_attn_Wk, L1_attn_Wv, L1_ffn_W, L1_ffn_b,
           L1_ffn_lora_down, L1_ffn_lora_up, L1_ffn_down_aux, L1_ffn_up_aux):
    f32 = lambda a: np.ascontiguousarray(np.asarray(a, dtype=np.float32))
    x = f32(L0_attn_x)
    common = dict(
        wqkv=f32(L0_attn_Wqkv), bqkv=f32(L0_attn_bqkv), wout=f32(L0_attn_Wout),
        bout=f32(L0_attn_bout), omega=f32(L0_attn_omega),
        w_gate=f32(L0_ffn_w_gate), w_up=f32(L0_ffn_w_up), w_down=f32(L0_ffn_w_down),
        wq1=f32(L1_attn_Wq), wk1=f32(L1_attn_Wk), wv1=f32(L1_attn_Wv),
        w1=f32(L1_ffn_W), b1=f32(L1_ffn_b), l_dn=f32(L1_ffn_lora_down),
        l_up=f32(L1_ffn_lora_up), d_aux=f32(L1_ffn_down_aux), u_aux=f32(L1_ffn_up_aux))
    nc = _get_program()
    in_maps = []
    for c in range(8):
        b, g = c // 2, c % 2
        xb = x[b] if g == 0 else np.ascontiguousarray(np.roll(x[b], -TO, axis=0))
        in_maps.append(dict(x_in=xb, **common))
    res = run_bass_kernel_spmd(nc, in_maps, core_ids=list(range(8)))
    out = np.empty((4, T, D), dtype=np.float32)
    for c in range(8):
        b, g = c // 2, c % 2
        out[b, g * TO:(g + 1) * TO, :] = np.asarray(res.results[c]["y_out"])
    return out
